# version 17
# baseline (speedup 1.0000x reference)
import math
import numpy as np
import concourse.bass as bass
import concourse.mybir as mybir
from concourse.bass_utils import run_bass_kernel_spmd

F32 = mybir.dt.float32
BF16 = mybir.dt.bfloat16
AF = mybir.ActivationFunctionType
ALU = mybir.AluOpType
AX = mybir.AxisListType

NCORE = 8
DEBUG_DUMPS = False
T = 1024
TT = 8
D = 4096
DC = 32
NH, NKV, HD = 32, 8, 128
GROUPS = NH // NKV
KT = 64
NEXP, R = 8, 16
HR = NEXP * R
FFN_COEF = 0.5 * (32.0 / 16.0)
INV_SQRT_HD = 1.0 / math.sqrt(HD)
EXP_BIAS = 20.0
LN_EPS = 1e-5


class Sem:
    def __init__(self, handle):
        self.h = handle
        self.n = 0

    def inc(self, k=1):
        self.n += k
        return self.h


def build_nc():
    nc = bass.Bass("TRN2", target_bir_lowering=False)

    x_in = nc.dram_tensor("x_in", [T, D], F32, kind="ExternalInput").ap()
    wq_d = nc.dram_tensor("wq_t", [NH, 128, DC * 128], F32, kind="ExternalInput").ap()
    wk_d = nc.dram_tensor("wk_t", [NKV, 128, DC * 128], F32, kind="ExternalInput").ap()
    wv_d = nc.dram_tensor("wv_t", [DC, 128, NKV * HD], F32, kind="ExternalInput").ap()
    wo_d = nc.dram_tensor("wo_t", [DC, 128, DC * 128], F32, kind="ExternalInput").ap()
    wra_d = nc.dram_tensor("wra", [DC, 128, NEXP + HR], F32, kind="ExternalInput").ap()
    btf_d = nc.dram_tensor("btf", [HR, D], F32, kind="ExternalInput").ap()
    br_d = nc.dram_tensor("brrow", [1, NEXP + HR], F32, kind="ExternalInput").ap()
    cos_d = nc.dram_tensor("cos_t", [HD, T], F32, kind="ExternalInput").ap()
    ssin_d = nc.dram_tensor("ssin_t", [HD, T], F32, kind="ExternalInput").ap()
    idn_d = nc.dram_tensor("idn", [128, 128], F32, kind="ExternalInput").ap()
    out_d = nc.dram_tensor("out", [T, D], F32, kind="ExternalOutput").ap()
    if DEBUG_DUMPS:
        dbg_x1 = nc.dram_tensor("dbg_x1", [T, D], F32, kind="ExternalOutput").ap()
        dbg_klv = nc.dram_tensor("dbg_klv", [2048, 1024], BF16, kind="ExternalOutput").ap()
        dbg_q = nc.dram_tensor("dbg_q", [NH * 128, T], BF16, kind="ExternalOutput").ap()
        dbg_xat = nc.dram_tensor("dbg_xat", [T, D], F32, kind="ExternalOutput").ap()
        dbg_ot = nc.dram_tensor("dbg_ot", [128, 32768], BF16, kind="ExternalOutput").ap()

    klv_loc_t = nc.dram_tensor("klv_loc", [2048, 1024], BF16)
    klv_g_t = nc.dram_tensor("klv_g", [NCORE * 2048 * 2, 1024], BF16)
    klv_loc = klv_loc_t.ap()
    klv_g_full = klv_g_t.ap()
    klv_g = klv_g_full[0:NCORE * 2048, :]
    q_loc = nc.dram_tensor("q_loc", [NH * 128, T], BF16).ap()
    x1_d = nc.dram_tensor("x1_d", [T, D], F32).ap()
    xat_d = nc.dram_tensor("xat_d", [T, D], F32).ap()

    from contextlib import ExitStack
    ctx = ExitStack()
    nc._kernel_ctx = ctx

    sbt = lambda name, shape, dt: ctx.enter_context(nc.sbuf_tensor(name, shape, dt))
    pst = lambda name, shape, dt: ctx.enter_context(nc.psum_tensor(name, shape, dt))

    blob1 = sbt("blob1", [128, 32768], BF16)
    blob2 = sbt("blob2", [128, 32768], BF16)
    kb0 = sbt("kb0", [128, 8192], BF16)
    kb1 = sbt("kb1", [128, 8192], BF16)
    qb = sbt("qb", [128, 8704], BF16)
    xb0 = sbt("xb0", [128, 4096], F32)
    ptile_t = sbt("ptile", [128, 3072], BF16)
    idn32 = sbt("idn32", [128, 128], F32)
    idnbf = sbt("idnbf", [128, 128], BF16)
    st_a = sbt("st_a", [128, 512], F32)
    st_b = sbt("st_b", [128, 512], F32)
    zinv_t = sbt("zinv", [128, 4], F32)
    onesb = sbt("onesb", [128, 1], BF16)
    onesf = sbt("onesf", [128, 1], F32)
    ones1b = sbt("ones1b", [1, 128], BF16)
    ones1f = sbt("ones1f", [1, 128], F32)
    scr8 = sbt("scr8", [128, 8], F32)
    stat = sbt("stat", [128, 16], F32)
    consts = sbt("consts", [128, 8], F32)

    CONST_VALS = [-EXP_BIAS, INV_SQRT_HD, LN_EPS, -0.5]
    for i, v in enumerate(CONST_VALS):
        nc.const_aps.register(F32, float(v), consts[:, i:i + 1])

    cosb = xb0.ap()[:, 0:1024]
    ssinb = xb0.ap()[:, 1024:2048]
    b2 = blob2.ap()
    qkst = st_a.ap().bitcast(BF16)
    lowS = b2[:, 16384:16512]
    lowST = b2[:, 16512:16640]
    ln_tok = b2[:, 0:4096]
    lnt_tile = b2[:, 4096:8192]
    x1t = b2[:, 8192:16384].bitcast(F32)
    otall = b2
    qbap = qb.ap()
    wra_sb = qbap[:, 0:DC * 136]
    btf_sb = qbap[:, DC * 136:DC * 136 + 4096]
    brr_sb = qbap[:, DC * 136 + 4096:DC * 136 + 4096 + 136]
    rope_c1 = qbap[:, 0:2048].bitcast(F32)
    rope_s1 = qbap[:, 2048:4096].bitcast(F32)
    qslot = [qbap[:, 0:4096], qbap[:, 4096:8192]]
    x2t = kb0.ap().bitcast(F32)
    xatt = kb1.ap().bitcast(F32)
    ptile = ptile_t.ap()
    zinv = zinv_t.ap()
    acc_sb = xb0.ap()[:, 2048:2816]
    zrow = st_a.ap()[0:1, 0:512]
    otc = st_b.ap()
    blob1f = blob1.ap().bitcast(F32)
    kb0f = kb0.ap().bitcast(F32)
    kb1f = kb1.ap().bitcast(F32)
    ln_tok1 = b2[:, 20480:24576]
    lnt_tile1 = b2[:, 24576:28672]

    ps_s = pst("ps_s", [128, 3072], F32).ap()
    ps_ot = pst("ps_ot", [128, 512], F32).ap()
    ps_z = pst("ps_z", [128, 512], F32).ap()
    ps_tr = [ps_s[:, 1536:2048].bitcast(BF16), ps_s[:, 2048:2560].bitcast(BF16)]
    ps_zb = ps_s[:, 2048:2560]
    ps_lw = ps_ot.bitcast(BF16)

    S = {}
    for name in ["q_x", "q_w", "q_sp", "c_pe", "c_pes", "c_pev", "c_exp",
                 "c_se", "c_ve", "c_veo", "c_vez", "cc", "cc2"]:
        S[name] = Sem(ctx.enter_context(nc.semaphore(name)))
    q_x, q_w, q_sp = S["q_x"], S["q_w"], S["q_sp"]
    c_pe, c_pes, c_pev = S["c_pe"], S["c_pes"], S["c_pev"]
    c_exp, c_se, c_ve, c_veo = S["c_exp"], S["c_se"], S["c_ve"], S["c_veo"]
    cc = S["cc"]
    cc2 = S["cc2"]
    c_vez = S["c_vez"]

    prog = {k: [] for k in ["sync", "gpsimd", "tensor", "vector", "scalar"]}

    def on(eng, fn):
        prog[eng].append(fn)

    def op(eng, fn, inc=None, k=1):
        if inc is not None:
            h = inc.inc(k)
            on(eng, lambda e, fn=fn, h=h, k=k: fn(e).then_inc(h, k))
        else:
            on(eng, lambda e, fn=fn: fn(e))

    def dma(eng, sem, out_ap, in_ap):
        op(eng, lambda e, o=out_ap, i=in_ap: e.dma_start(out=o, in_=i), inc=sem, k=16)

    _last_wait = {}
    _future_waits = []

    def wait(eng, sem, val=None, future=False):
        v = sem.n if val is None else val
        if future:
            _future_waits.append((eng, sem, v))
        else:
            assert v <= sem.n, f"future wait on {eng}: {v} > {sem.n}"
        key = (eng, id(sem))
        if v > 0 and _last_wait.get(key, 0) < v:
            _last_wait[key] = v
            on(eng, lambda e, h=sem.h, v=v: e.wait_ge(h, v))

    stg_last = {"a": 0, "b": 0}

    op("vector", lambda v: v.memset(onesb[:], 1.0))
    op("vector", lambda v: v.memset(onesf[:], 1.0))
    op("vector", lambda v: v.memset(ones1b[:], 1.0))
    op("vector", lambda v: v.memset(ones1f[:], 1.0))
    for i, val in enumerate(CONST_VALS):
        op("vector", lambda v, i=i, val=val: v.memset(consts[:, i:i + 1], float(val)))
    op("vector", lambda v: v.memset(brr_sb[0:1, :], 0.0), inc=c_ve)

    dma("sync", q_x, idn32[:], idn_d)
    wait("vector", q_x)
    op("vector", lambda v: v.tensor_copy(idnbf[:], idn32[:]), inc=c_ve)
    VE_SETUP = c_ve.n

    for dc in range(DC):
        dma("gpsimd", q_w, wra_sb[:, dc * 136:(dc + 1) * 136], wra_d[dc])
    dma("gpsimd", q_w, btf_sb, btf_d)
    wait("gpsimd", c_ve, VE_SETUP)
    dma("gpsimd", q_w, brr_sb[0:1, 0:136], br_d)
    QW_FFN = q_w.n


    ln_stat = {"free": 0}

    def emit_ln(src_f32, dst_bf, extra_scalar_wait=None, dst_war=None, src_ok=None):
        sums, ssq = stat[:, 0:1], stat[:, 1:2]
        mcol, msq = stat[:, 2:3], stat[:, 3:4]
        vcol, lncol = stat[:, 4:5], stat[:, 5:6]
        rstd, nmr = stat[:, 6:7], stat[:, 7:8]
        if extra_scalar_wait is not None:
            extra_scalar_wait()
        if dst_war is not None:
            wait("scalar", c_pe, dst_war)
            wait("vector", c_pe, dst_war)
        if src_ok is not None:
            wait("scalar", c_ve, max(src_ok, ln_stat["free"]))
        else:
            wait("scalar", c_ve)
        op("scalar", lambda s: s.activation(dst_bf, src_f32, AF.Identity, accum_out=sums), inc=c_se)
        op("scalar", lambda s: s.activation(dst_bf, src_f32, AF.Square, accum_out=ssq), inc=c_se)
        wait("vector", c_se)
        op("vector", lambda v: v.tensor_scalar(mcol, sums, 1.0 / D, None, ALU.mult), inc=c_ve)
        op("vector", lambda v: v.tensor_scalar(msq, ssq, 1.0 / D, None, ALU.mult), inc=c_ve)
        ln_stat["free"] = c_ve.n
        wait("vector", c_ve)
        op("vector", lambda v: v.tensor_mul(vcol, mcol, mcol), inc=c_ve)
        wait("vector", c_ve)
        op("vector", lambda v: v.tensor_sub(vcol, msq, vcol), inc=c_ve)
        wait("scalar", c_ve)
        op("scalar", lambda s: s.activation(lncol, vcol, AF.Ln, bias=LN_EPS), inc=c_se)
        wait("scalar", c_se)
        op("scalar", lambda s: s.activation(rstd, lncol, AF.Exp, scale=-0.5), inc=c_se)
        wait("vector", c_se)
        op("vector", lambda v: v.tensor_scalar(nmr, mcol, rstd, -1.0, ALU.mult, ALU.mult), inc=c_ve)
        wait("vector", c_ve)
        op("vector", lambda v: v.tensor_scalar(dst_bf, src_f32, rstd, nmr, ALU.mult, ALU.add), inc=c_ve)

    def emit_transposes(src_bf, dst_fn, evac_war=None, src_ready=None):
        if src_ready is None:
            src_ready = c_ve.n
        wait("tensor", c_veo)
        if evac_war is not None:
            wait("vector", c_pe, evac_war)
        for g in range(4):
            slot = ps_tr[g % 2]
            wait("tensor", c_ve, src_ready)
            if g >= 2:
                wait("tensor", c_veo)
            for j in range(8):
                dc = g * 8 + j
                kwargs = dict(inc=c_pe) if j == 7 else {}
                op("tensor", lambda t, dc=dc, j=j, slot=slot:
                    t.transpose(slot[:, j * 128:(j + 1) * 128],
                                src_bf[:, dc * 128:(dc + 1) * 128], idnbf[:]), **kwargs)
            wait("vector", c_pe)
            op("vector", lambda v, g=g, slot=slot:
                v.tensor_copy(dst_fn(g), slot.rearrange("p (a b) -> p a b", a=8)),
                inc=c_veo)

    def emit_ffn(lnt, out_writer, qw_target, ce_ok=None):
        rt = ps_s[:, 0:136]
        if ce_ok is not None:
            wait("tensor", c_ve, ce_ok)
        else:
            wait("tensor", c_ve)
        wait("tensor", c_veo)
        wait("tensor", q_w, qw_target)
        for dc in range(DC):
            op("tensor", lambda t, dc=dc:
                t.matmul(rt, lnt[:, dc * 128:(dc + 1) * 128],
                         wra_sb[:, dc * 136:(dc + 1) * 136], start=(dc == 0), stop=False))
        op("tensor", lambda t:
            t.matmul(rt, ones1b[0:1, :], brr_sb[0:1, 0:136], start=False, stop=True), inc=c_pe)
        rt_done = c_pe.n
        mx, nmx, sm, rs, sc2 = stat[:, 8:9], stat[:, 9:10], stat[:, 10:11], stat[:, 11:12], stat[:, 12:13]
        wait("vector", c_pe)
        op("vector", lambda v: v.reduce_max(mx, rt[:, 0:8], axis=AX.X), inc=c_ve)
        wait("vector", c_ve)
        op("vector", lambda v: v.tensor_scalar(nmx, mx, -1.0, None, ALU.mult), inc=c_ve)
        wait("scalar", c_ve)
        op("scalar", lambda s: s.activation(scr8[:], rt[:, 0:8], AF.Exp, bias=nmx), inc=c_se)
        wait("vector", c_se)
        op("vector", lambda v: v.reduce_sum(sm, scr8[:], axis=AX.X), inc=c_ve)
        wait("vector", c_ve)
        op("vector", lambda v: v.reciprocal(rs, sm), inc=c_ve)
        for h in range(NEXP):
            wait("vector", c_ve)
            op("vector", lambda v, h=h:
                v.tensor_scalar(sc2, scr8[:, h:h + 1], rs, 8.0, ALU.mult, ALU.mult), inc=c_ve)
            wait("vector", c_ve)
            op("vector", lambda v, h=h:
                v.tensor_scalar(lowS[:, h * 16:(h + 1) * 16], rt[:, 8 + h * 16:8 + (h + 1) * 16],
                                sc2, None, ALU.mult), inc=c_ve)
        wait("tensor", c_ve)
        op("tensor", lambda t: t.transpose(ps_lw[:, 0:128], lowS, idnbf[:]), inc=c_pe)
        wait("vector", c_pe)
        op("vector", lambda v: v.tensor_copy(lowST, ps_lw[:, 0:128]), inc=c_ve)
        wait("tensor", c_ve)
        evac_done = {}
        for oc in range(8):
            slot = ps_s[:, 512 + (oc % 2) * 512:1024 + (oc % 2) * 512]
            if oc >= 2:
                wait("tensor", c_ve, evac_done[oc - 2])
            op("tensor", lambda t, slot=slot, oc=oc:
                t.matmul(slot, lowST, btf_sb[:, oc * 512:(oc + 1) * 512],
                         start=True, stop=True), inc=c_pe)
            wait("vector", c_pe)
            out_writer(oc, slot)
            evac_done[oc] = c_ve.n
        return rt_done

    lnb = blob1.ap()
    xs_p1 = [xb0.ap()[:, 0:4096], kb1f[:, 0:4096]]
    x1s_p1 = [x1t, kb0f[:, 0:4096]]
    lns = [(ln_tok, lnt_tile), (ln_tok1, lnt_tile1)]
    p1_x_cons = {}
    p1_tr2_done = {}
    p1_rt_done = {}

    p1_ln_ve = {}
    p1_back_ve = {}

    def p1_front(tt):
        s = tt % 2
        xcur = xs_p1[s]
        lt = lns[s][0]
        if tt >= 2:
            wait("sync", c_ve, p1_x_cons[tt - 2])
        dma("sync", q_x, xcur, x_in[tt * 128:(tt + 1) * 128, :])
        wait("vector", q_x)
        wait("scalar", q_x)
        if tt == 0:
            wait("scalar", c_ve, VE_SETUP)
        emit_ln(xcur, lt, dst_war=p1_tr2_done.get(tt - 2), src_ok=0)
        p1_ln_ve[tt] = c_ve.n

    def p1_back(tt):
        s = tt % 2
        xcur, x1cur = xs_p1[s], x1s_p1[s]
        lt, ltt = lns[s]
        emit_transposes(lt, lambda g, ltt=ltt: ltt[:, g * 1024:(g + 1) * 1024]
                        .rearrange("p (a b) -> p a b", a=8),
                        evac_war=p1_rt_done.get(tt - 2), src_ready=p1_ln_ve[tt])

        def p1_writer(oc, slot, tt=tt, xcur=xcur, x1cur=x1cur):
            key = "a" if oc % 2 == 0 else "b"
            st = st_a if oc % 2 == 0 else st_b
            op("vector", lambda v, oc=oc, slot=slot:
                v.scalar_tensor_tensor(x1cur[:, oc * 512:(oc + 1) * 512], slot, FFN_COEF,
                                       xcur[:, oc * 512:(oc + 1) * 512], ALU.mult, ALU.add),
                inc=c_ve)
            wait("vector", c_ve)
            wait("vector", q_sp, stg_last[key])
            op("vector", lambda v, st=st, oc=oc:
                v.tensor_copy(st[:], x1cur[:, oc * 512:(oc + 1) * 512]), inc=c_ve)
            wait("sync", c_ve)
            dma("sync", q_sp, x1_d[tt * 128:(tt + 1) * 128, oc * 512:(oc + 1) * 512], st[:])
            stg_last[key] = q_sp.n
        p1_rt_done[tt] = emit_ffn(ltt, p1_writer, QW_FFN,
                                  ce_ok=p1_back_ve.get(tt - 1))
        p1_x_cons[tt] = c_ve.n

        emit_ln(x1cur, lt)

        def lnb_dst(g, tt=tt):
            return lnb.rearrange("p (dc t) -> p dc t", dc=DC)[:, g * 8:(g + 1) * 8,
                                                             tt * 128:(tt + 1) * 128]
        emit_transposes(lt, lnb_dst)
        p1_tr2_done[tt] = c_pe.n
        p1_back_ve[tt] = c_ve.n

    for tt in range(TT):
        p1_front(tt)
        if tt >= 1:
            p1_back(tt - 1)
    p1_back(TT - 1)

    VE_P1 = c_ve.n
    VEO_P1 = c_veo.n

    wvt = blob2.ap()
    wait("gpsimd", c_ve, VE_P1)
    wait("gpsimd", c_veo, VEO_P1)
    for dc in range(DC):
        dma("gpsimd", q_w, wvt[:, dc * 1024:(dc + 1) * 1024], wv_d[dc])
    QW_WVT = q_w.n

    def v_bank(t2):
        if t2 < 6:
            return ps_s[:, t2 * 512:(t2 + 1) * 512]
        if t2 == 6:
            return ps_ot
        return ps_z

    for vh in range(2):
        wait("tensor", q_w, QW_WVT)
        wait("tensor", c_veo, VEO_P1)
        wait("tensor", c_ve)
        for dc in range(DC):
            for t2 in range(TT):
                kwargs = dict(inc=c_pe) if (dc == DC - 1 and t2 == TT - 1) else {}
                op("tensor", lambda t, dc=dc, t2=t2, vh=vh:
                    t.matmul(v_bank(t2),
                             lnb[:, dc * 1024 + t2 * 128:dc * 1024 + (t2 + 1) * 128],
                             wvt[:, dc * 1024 + vh * 512:dc * 1024 + (vh + 1) * 512],
                             start=(dc == 0), stop=(dc == DC - 1)), **kwargs)
        wait("vector", c_pe)
        for t2 in range(TT):
            key = "a" if t2 % 2 == 0 else "b"
            st = st_a if t2 % 2 == 0 else st_b
            wait("vector", q_sp, stg_last[key])
            op("vector", lambda v, t2=t2, st=st:
                v.tensor_copy(st[:].bitcast(BF16)[:, 0:512], v_bank(t2)), inc=c_ve)
            wait("sync", c_ve)
            dma("sync", q_sp,
                klv_loc[1024 + t2 * 128:1024 + (t2 + 1) * 128, vh * 512:(vh + 1) * 512],
                st[:].bitcast(BF16)[:, 0:512])
            stg_last[key] = q_sp.n

    wait("sync", c_ve, VE_P1)
    dma("sync", q_x, cosb, cos_d)
    dma("sync", q_x, ssinb, ssin_d)
    wblk = [kb0.ap()[:, 0:4096], kb1.ap()[:, 0:4096]]
    rope_free = {0: 0, 1: 0}
    blk_free = {0: 0, 1: 0}
    qk_store_last = 0

    def emit_qk_head(par, w_src, dst_rows, first):
        nonlocal qk_store_last
        wait("gpsimd", c_pe, blk_free[par])
        dma("gpsimd", q_w, wblk[par], w_src)
        qwn = q_w.n
        psv = ps_s[:, par * 1024:(par + 1) * 1024]
        wait("tensor", q_w, qwn)
        if first:
            wait("tensor", c_ve)
            wait("tensor", c_veo, VEO_P1)
        wait("tensor", c_ve, rope_free[par])
        for half in range(2):
            for dc in range(DC):
                kwargs = dict(inc=c_pe) if (dc == DC - 1 and half == 1) else {}
                op("tensor", lambda t, dc=dc, half=half, par=par:
                    t.matmul(psv[:, half * 512:(half + 1) * 512],
                             wblk[par][:, dc * 128:(dc + 1) * 128],
                             lnb[:, dc * 1024 + half * 512:dc * 1024 + (half + 1) * 512],
                             start=(dc == 0), stop=(dc == DC - 1)), **kwargs)
        blk_free[par] = c_pe.n
        wait("vector", c_pe)
        wait("vector", q_x)
        op("vector", lambda v, par=par: v.tensor_mul(rope_c1, psv, cosb), inc=c_ve)
        op("vector", lambda v, par=par:
            v.tensor_mul(rope_s1[0:64, :], psv[64:128, :], ssinb[0:64, :]), inc=c_ve)
        op("vector", lambda v, par=par:
            v.tensor_mul(rope_s1[64:128, :], psv[0:64, :], ssinb[64:128, :]), inc=c_ve)
        wait("vector", c_ve)
        wait("vector", q_sp, max(qk_store_last, stg_last["a"]))
        op("vector", lambda v: v.tensor_add(qkst, rope_c1, rope_s1), inc=c_ve)
        rope_free[par] = c_ve.n
        wait("sync", c_ve)
        dma("sync", q_sp, dst_rows, qkst)
        qk_store_last = q_sp.n

    for kh in range(NKV):
        emit_qk_head(kh % 2, wk_d[kh], klv_loc[kh * 128:(kh + 1) * 128, :], first=(kh == 0))
    KLV_DONE = q_sp.n

    for qh in range(NH):
        emit_qk_head(qh % 2, wq_d[qh], q_loc[qh * 128:(qh + 1) * 128, :], first=False)

    wait("gpsimd", q_sp, KLV_DONE)
    op("gpsimd", lambda g:
        g.collective_compute("AllGather", ALU.bypass, replica_groups=[list(range(NCORE))],
                             ins=[klv_loc[0:1024, :].opt()],
                             outs=[klv_g_full[0:NCORE * 1024, :].opt()]), inc=cc)
    op("gpsimd", lambda g:
        g.collective_compute("AllGather", ALU.bypass, replica_groups=[list(range(NCORE))],
                             ins=[klv_loc[1024:2048, :].opt()],
                             outs=[klv_g_full[NCORE * 1024:NCORE * 2048, :].opt()]), inc=cc2)

    PE_P2 = c_pe.n
    VE_P2 = c_ve.n
    QSP_P2 = q_sp.n

    def v_view(h, kt):
        base = ((h // 2) % 2) * 16384
        return blob1.ap()[:, base + kt * 256 + (h % 2) * 128:base + kt * 256 + (h % 2) * 128 + 128]

    def k_view(h, kt):
        return (kb0 if h % 2 == 0 else kb1).ap()[:, kt * 128:(kt + 1) * 128]

    def load_v_gang(g):
        base = ((g % 2)) * 16384
        for r in range(NCORE):
            src = klv_g[NCORE * 1024 + r * 1024:NCORE * 1024 + (r + 1) * 1024,
                        g * 256:(g + 1) * 256]
            src = src.rearrange("(kt p) c -> p kt c", p=128)
            dst = blob1.ap()[:, base + r * 8 * 256:base + (r + 1) * 8 * 256]
            dst = dst.rearrange("p (kt c) -> p kt c", c=256)
            dma("sync", q_x, dst, src)

    def load_k_head(h):
        dst = (kb0 if h % 2 == 0 else kb1).ap()
        for r in range(NCORE):
            dma("sync", q_x, dst[:, r * 1024:(r + 1) * 1024],
                klv_g[r * 1024 + h * 128:r * 1024 + (h + 1) * 128, :])

    def load_q_head(h):
        dma("sync", q_x, qslot[h % 2].rearrange("p (q t) -> p q t", q=4),
            q_loc[h * 4 * 128:(h + 1) * 4 * 128, :].rearrange("(q p) t -> p q t", p=128))

    wait("sync", cc, 1)
    wait("sync", c_pe, PE_P2)
    wait("sync", c_ve, VE_P2)
    wait("sync", q_sp, QSP_P2)
    load_k_head(0)
    load_q_head(0)
    kq_ready = {0: q_x.n}
    wait("sync", cc2, 1)
    load_v_gang(0)
    vg_ready = {0: q_x.n}
    load_v_gang(1)
    vg_ready[1] = q_x.n
    head_loads = {0: q_x.n}
    pending_tail = []
    otc_free = {}
    head_s_done = {}
    gang_pv_done = {}
    ot_free = {}
    unit = 0

    def emit_pv_z(h, ktd, exp_base):
        j = ktd // 2
        prhs = ptile[:, (j % 3) * 1024 + (ktd % 2) * 512:(j % 3) * 1024 + (ktd % 2) * 512 + 512]
        op("tensor", lambda t, h=h, ktd=ktd, prhs=prhs:
            t.matmul(ps_ot, v_view(h, ktd), prhs, start=(ktd == 0), stop=(ktd == KT - 1)))
        for j2 in range(4):
            kwargs = dict(inc=c_pev) if (j2 == 3 and ktd % 2 == 1) else {}
            op("tensor", lambda t, j2=j2, prhs=prhs, ktd=ktd:
                t.matmul(ps_z[:, j2:j2 + 1], prhs[:, j2 * 128:(j2 + 1) * 128], onesb[:],
                         start=False, stop=(ktd == KT - 1 and j2 == 3)),
                **kwargs)

    def emit_s(h, qv, kt, exp_base, u):
        if kt >= 6:
            wait("tensor", c_exp, exp_base + (kt - 6) // 2 + 1, future=True)
        else:
            wait("tensor", c_exp, exp_base)
        if u == 0 and kt == 0:
            wait("tensor", c_ve, VE_P2)
        sv = ps_s[:, (kt % 6) * 512:(kt % 6) * 512 + 512]
        kwargs = dict(inc=c_pes) if kt % 2 == 1 else {}
        op("tensor", lambda t, h=h, kt=kt, sv=sv, qv=qv:
            t.matmul(sv, k_view(h, kt), qv, start=True, stop=True), **kwargs)

    for h in range(NKV):
        wait("tensor", q_x, kq_ready[h])
        if h == 0:
            wait("tensor", q_sp, QSP_P2)
        for ql in range(GROUPS):
            qh = h * GROUPS + ql
            for half in range(2):
                u = unit
                qv = qslot[h % 2][:, ql * 1024 + half * 512:ql * 1024 + half * 512 + 512]
                exp_base = c_exp.n
                pes_base = c_pes.n
                pev_base = c_pev.n
                vez_base = c_vez.n
                op("vector", lambda v: v.memset(ps_z[:, 0:4], 0.0), inc=c_ve)
                zmemset_ve = c_ve.n
                for kt in range(4):
                    emit_s(h, qv, kt, exp_base, u)
                if pending_tail:
                    pending_tail.pop(0)()
                    ot_free[u - 1] = c_veo.n
                if u >= 1:
                    wait("tensor", c_veo, ot_free[u - 1])
                for kt in (4, 5):
                    emit_s(h, qv, kt, exp_base, u)
                for ktd in range(KT):
                    if ktd == 0:
                        if u >= 1:
                            wait("tensor", c_ve, otc_free[u - 1])
                        else:
                            wait("tensor", c_veo)
                        wait("tensor", c_ve, zmemset_ve)
                        wait("tensor", q_x, vg_ready[h // 2])
                    if ktd % 2 == 0:
                        wait("tensor", c_exp, exp_base + ktd // 2 + 1, future=True)
                    emit_pv_z(h, ktd, exp_base)
                    if ktd + 6 < KT:
                        emit_s(h, qv, ktd + 6, exp_base, u)
                for j in range(KT // 2):
                    pslot = ptile[:, (j % 3) * 1024:(j % 3) * 1024 + 1024]
                    sj = ps_s[:, (2 * j % 6) * 512:(2 * j % 6) * 512 + 1024]
                    wait("scalar", c_pes, pes_base + j + 1)
                    pw = pev_base + (j - 3) + 1
                    if pw > 0:
                        wait("scalar", c_pev, pw)
                    op("scalar", lambda s, sj=sj, pslot=pslot:
                        s.activation(pslot, sj, AF.Exp, bias=-EXP_BIAS,
                                     scale=INV_SQRT_HD), inc=c_exp)
                wait("vector", c_pev, pev_base + KT // 2)
                op("vector", lambda v: v.tensor_copy(otc[:], ps_ot), inc=c_ve)
                otc_free[u] = c_ve.n
                op("vector", lambda v: v.reciprocal(zinv[:], ps_z[:, 0:4]), inc=c_ve)
                zinv_ve = c_ve.n

                def unit_tail(qh=qh, half=half, zinv_ve=zinv_ve):
                    wait("tensor", c_ve, zinv_ve)
                    for j2 in range(4):
                        kwargs = dict(inc=c_pe) if j2 == 3 else {}
                        op("tensor", lambda t, j2=j2:
                            t.transpose(ps_zb[0:1, j2 * 128:(j2 + 1) * 128],
                                        zinv[:, j2:j2 + 1], idn32[:]), **kwargs)
                    wait("vector", c_pe)
                    op("vector", lambda v: v.tensor_copy(zrow[0:1, :], ps_zb[0:1, 0:512]), inc=c_ve)
                    zrow_ve = c_ve.n
                    wait("tensor", c_ve, zrow_ve)
                    op("tensor", lambda t:
                        t.matmul(ps_zb[:, 0:512], ones1f[0:1, :], zrow[0:1, :],
                                 start=True, stop=True), inc=c_pe)
                    wait("vector", c_pe)
                    op("vector", lambda v, qh=qh, half=half:
                        v.tensor_mul(otall[:, qh * 1024 + half * 512:qh * 1024 + half * 512 + 512],
                                     otc[:], ps_zb[:, 0:512]), inc=c_veo)
                pending_tail.append(unit_tail)
                unit += 1
        head_s_done[h] = c_pes.n
        if h % 2 == 1:
            gang_pv_done[h // 2] = c_pev.n
        if h + 1 < NKV:
            nh_ = h + 1
            if nh_ >= 2:
                wait("sync", c_pes, head_s_done[nh_ - 2])
            load_k_head(nh_)
            load_q_head(nh_)
            kq_ready[nh_] = q_x.n
            if h % 2 == 1:
                g_next = h // 2 + 2
                if g_next <= 3:
                    wait("sync", c_pev, gang_pv_done[g_next - 2])
                    load_v_gang(g_next)
                    vg_ready[g_next] = q_x.n

    if pending_tail:
        pending_tail.pop(0)()
        ot_free[unit - 1] = c_veo.n
    PE_P3 = c_pe.n
    PES_P3 = c_pes.n
    VEO_P3 = c_veo.n

    wo_blk_free = {0: ("pes", head_s_done[NKV - 2]), 1: ("pes", head_s_done[NKV - 1])}
    acc_free = {}
    first_sta = True

    for doc in range(DC):
        par = doc % 2
        sem_kind, val = wo_blk_free[par]
        wait("gpsimd", c_pes if sem_kind == "pes" else c_pe, val)
        dma("gpsimd", q_w, wblk[par], wo_d[doc])
        qwn = q_w.n
        for half in range(2):
            sidx = (doc * 2 + half) % 4
            slot = ps_s[:, sidx * 512:(sidx + 1) * 512]
            wait("tensor", q_w, qwn)
            if doc * 2 + half >= 4:
                wait("tensor", c_ve, acc_free[sidx])
            elif doc == 0 and half == 0:
                wait("tensor", c_veo, VEO_P3)
                wait("tensor", c_exp)
                wait("tensor", c_pev)
            for cci in range(DC):
                kwargs = dict(inc=c_pe) if cci == DC - 1 else {}
                op("tensor", lambda t, doc=doc, half=half, cci=cci, slot=slot, par=par:
                    t.matmul(slot, wblk[par][:, cci * 128:(cci + 1) * 128],
                             otall[:, cci * 1024 + half * 512:cci * 1024 + half * 512 + 512],
                             start=(cci == 0), stop=(cci == DC - 1)), **kwargs)
            if half == 1:
                wo_blk_free[par] = ("pe", c_pe.n)
            wait("vector", c_pe)
            if first_sta:
                wait("vector", q_sp, stg_last["a"])
                first_sta = False
            op("vector", lambda v, slot=slot: v.tensor_copy(st_a[:], slot), inc=c_ve)
            acc_free[sidx] = c_ve.n
            wait("tensor", c_ve)
            trb = ps_ot if (doc * 2 + half) % 2 == 0 else ps_z
            for j in range(4):
                kwargs = dict(inc=c_pe) if j == 3 else {}
                op("tensor", lambda t, j=j, trb=trb:
                    t.transpose(trb[0:1 * 128, j * 128:(j + 1) * 128],
                                st_a[:, j * 128:(j + 1) * 128], idn32[:]), **kwargs)
            wait("vector", c_pe)
            wait("vector", q_sp, stg_last["b"])
            op("vector", lambda v, trb=trb:
                v.tensor_copy(st_b[:], trb[:, 0:512]), inc=c_ve)
            wait("sync", c_ve)
            dma("sync", q_sp,
                xat_d[half * 512:(half + 1) * 512, doc * 128:(doc + 1) * 128]
                .rearrange("(j p) c -> p j c", p=128),
                st_b[:].rearrange("p (j c) -> p j c", c=128))
            stg_last["b"] = q_sp.n

    PE_P4 = c_pe.n
    VE_P4 = c_ve.n
    QSP_P4 = q_sp.n

    wait("gpsimd", c_pes, PES_P3)
    for dc in range(DC):
        dma("gpsimd", q_w, wra_sb[:, dc * 136:(dc + 1) * 136], wra_d[dc])
    dma("gpsimd", q_w, btf_sb, btf_d)
    op("gpsimd", lambda g: g.memset(brr_sb[0:1, :], 0.0))
    dma("gpsimd", q_w, brr_sb[0:1, 0:136], br_d)
    QW_FFN2 = q_w.n
    globals_qwffn = QW_FFN2

    x1s_p5 = [xb0.ap()[:, 0:4096], blob1f[:, 0:4096]]
    xas_p5 = [xatt[:, 0:4096], blob1f[:, 4096:8192]]
    x2s_p5 = [x2t[:, 0:4096], blob1f[:, 8192:12288]]
    p5_x_cons = {}
    p5_tr_done = {}
    p5_rt_done = {}

    p5_ln_ve = {}
    p5_back_ve = {}

    def p5_front(tt):
        s = tt % 2
        x1cur, xacur, x2cur = x1s_p5[s], xas_p5[s], x2s_p5[s]
        lt = lns[s][0]
        if tt < 2:
            wait("sync", c_pe, PE_P4)
            wait("sync", c_ve, VE_P4)
            wait("sync", q_sp, QSP_P4)
        else:
            wait("sync", c_ve, p5_x_cons[tt - 2])
        dma("sync", q_x, x1cur, x1_d[tt * 128:(tt + 1) * 128, :])
        dma("sync", q_x, xacur, xat_d[tt * 128:(tt + 1) * 128, :])
        wait("vector", q_x)
        wait("vector", q_w, QW_FFN2)
        if tt < 2:
            wait("vector", c_pe, PE_P4)
        op("vector", lambda v, a=x1cur, b=xacur, o=x2cur:
            v.tensor_add(o, a, b), inc=c_ve)
        p5_x_cons[tt] = c_ve.n
        wait("scalar", q_w, QW_FFN2)
        if tt < 2:
            wait("scalar", c_pe, PE_P4)
        emit_ln(x2cur, lt, dst_war=p5_tr_done.get(tt - 2), src_ok=p5_x_cons[tt])
        p5_ln_ve[tt] = c_ve.n

    def p5_back(tt):
        s = tt % 2
        x2cur = x2s_p5[s]
        lt, ltt = lns[s]
        emit_transposes(lt, lambda g, ltt=ltt: ltt[:, g * 1024:(g + 1) * 1024]
                        .rearrange("p (a b) -> p a b", a=8),
                        evac_war=p5_rt_done.get(tt - 2), src_ready=p5_ln_ve[tt])
        p5_tr_done[tt] = c_pe.n

        def p5_writer(oc, slot, tt=tt, x2cur=x2cur):
            key = "a" if oc % 2 == 0 else "b"
            st = st_a if oc % 2 == 0 else st_b
            wait("vector", q_sp, stg_last[key])
            op("vector", lambda v, oc=oc, slot=slot:
                v.scalar_tensor_tensor(st[:], slot, FFN_COEF,
                                       x2cur[:, oc * 512:(oc + 1) * 512], ALU.mult, ALU.add),
                inc=c_ve)
            wait("sync", c_ve)
            dma("sync", q_sp, out_d[tt * 128:(tt + 1) * 128, oc * 512:(oc + 1) * 512], st[:])
            stg_last[key] = q_sp.n
        p5_rt_done[tt] = emit_ffn(ltt, p5_writer, QW_FFN2,
                                  ce_ok=p5_back_ve.get(tt - 1))
        p5_back_ve[tt] = c_ve.n

    for tt in range(TT):
        p5_front(tt)
        if tt >= 1:
            p5_back(tt - 1)
    p5_back(TT - 1)

    if DEBUG_DUMPS:
        q_dbg = Sem(ctx.enter_context(nc.semaphore("q_dbg")))
        wait("gpsimd", q_sp)
        wait("gpsimd", c_veo)
        wait("gpsimd", c_ve)
        dma("gpsimd", q_dbg, dbg_x1, x1_d)
        dma("gpsimd", q_dbg, dbg_klv, klv_loc)
        dma("gpsimd", q_dbg, dbg_q, q_loc)
        dma("gpsimd", q_dbg, dbg_xat, xat_d)
        dma("gpsimd", q_dbg, dbg_ot, otall)
        wait("gpsimd", q_dbg)
    wait("gpsimd", q_sp)

    for eng_, sem_, v_ in _future_waits:
        assert v_ <= sem_.n, f"unsatisfiable future wait {eng_} {v_} > {sem_.n}"

    with nc.Block() as block:
        @block.sync
        def _(eng):
            for fn in prog["sync"]:
                fn(eng)

        @block.gpsimd
        def _(eng):
            for fn in prog["gpsimd"]:
                fn(eng)

        @block.tensor
        def _(eng):
            for fn in prog["tensor"]:
                fn(eng)

        @block.vector
        def _(eng):
            for fn in prog["vector"]:
                fn(eng)

        @block.scalar
        def _(eng):
            for fn in prog["scalar"]:
                fn(eng)

    return nc


def _host_prep(L0_attn_hidden_states, L0_attn_Wq, L0_attn_Wk, L0_attn_Wv,
               L0_attn_Wo, L0_ffn_A, L0_ffn_B, L0_ffn_Wr, L0_ffn_br):
    f32 = np.float32
    x = np.ascontiguousarray(np.asarray(L0_attn_hidden_states, f32).reshape(8192, D))
    Wq = np.asarray(L0_attn_Wq, f32)
    Wk = np.asarray(L0_attn_Wk, f32)
    Wv = np.asarray(L0_attn_Wv, f32)
    Wo = np.asarray(L0_attn_Wo, f32)
    A = np.asarray(L0_ffn_A, f32)
    B = np.asarray(L0_ffn_B, f32)
    Wr = np.asarray(L0_ffn_Wr, f32)
    br = np.asarray(L0_ffn_br, f32)

    wq_t = np.ascontiguousarray(
        Wq.T.reshape(DC, 128, NH, 128).transpose(2, 1, 0, 3).reshape(NH, 128, DC * 128))
    wk_t = np.ascontiguousarray(
        Wk.T.reshape(DC, 128, NKV, 128).transpose(2, 1, 0, 3).reshape(NKV, 128, DC * 128))
    wv_t = np.ascontiguousarray(Wv.T.reshape(DC, 128, NKV * HD))
    wo_t = np.ascontiguousarray(
        Wo.T.reshape(DC, 128, DC, 128).transpose(2, 1, 0, 3).reshape(DC, 128, DC * 128))
    wra = np.ascontiguousarray(
        np.concatenate([Wr.T, A.transpose(2, 0, 1).reshape(D, HR)], axis=1)
        .reshape(DC, 128, NEXP + HR))
    btf = np.ascontiguousarray(B.transpose(0, 2, 1).reshape(HR, D))
    brrow = np.zeros((1, NEXP + HR), f32)
    brrow[0, :NEXP] = br

    pos = np.arange(8192, dtype=np.float64)
    inv = 1.0 / (500000.0 ** (np.arange(0, HD, 2, dtype=np.float64) / HD))
    freqs = np.outer(pos, inv)
    emb = np.concatenate([freqs, freqs], axis=1)
    cosT = np.cos(emb).T.astype(f32)
    sinT = np.sin(emb).T.astype(f32)
    ssinT = sinT.copy()
    ssinT[:64, :] *= -1.0
    idn = np.eye(128, dtype=f32)

    per_core = []
    for i in range(NCORE):
        sl = slice(i * T, (i + 1) * T)
        per_core.append({
            "x_in": np.ascontiguousarray(x[sl]),
            "wq_t": wq_t, "wk_t": wk_t, "wv_t": wv_t, "wo_t": wo_t,
            "wra": wra, "btf": btf, "brrow": brrow,
            "cos_t": np.ascontiguousarray(cosT[:, sl]),
            "ssin_t": np.ascontiguousarray(ssinT[:, sl]),
            "idn": idn,
        })
    return per_core


_NC_CACHE = {}


def kernel(**inputs):
    in_maps = _host_prep(**inputs)
    if "nc" not in _NC_CACHE:
        _NC_CACHE["nc"] = build_nc()
    nc = _NC_CACHE["nc"]
    res = run_bass_kernel_spmd(nc, in_maps, core_ids=list(range(NCORE)))
    out = np.concatenate([res.results[i]["out"] for i in range(NCORE)], axis=0)
    return out.reshape(1, 8192, D).astype(np.float32)


if __name__ == "__main__":
    nc = build_nc()
    print("built ok:", nc.n_instructions() if hasattr(nc, "n_instructions") else "?")


# version 19 (final)
# speedup vs baseline: 1.0081x; 1.0081x over previous
import math
import numpy as np
import concourse.bass as bass
import concourse.mybir as mybir
from concourse.bass_utils import run_bass_kernel_spmd

F32 = mybir.dt.float32
BF16 = mybir.dt.bfloat16
AF = mybir.ActivationFunctionType
ALU = mybir.AluOpType
AX = mybir.AxisListType

NCORE = 8
DEBUG_DUMPS = False
T = 1024
TT = 8
D = 4096
DC = 32
NH, NKV, HD = 32, 8, 128
GROUPS = NH // NKV
KT = 64
NEXP, R = 8, 16
HR = NEXP * R
FFN_COEF = 0.5 * (32.0 / 16.0)
INV_SQRT_HD = 1.0 / math.sqrt(HD)
EXP_BIAS = 20.0
LN_EPS = 1e-5


class Sem:
    def __init__(self, handle):
        self.h = handle
        self.n = 0

    def inc(self, k=1):
        self.n += k
        return self.h


def build_nc():
    nc = bass.Bass("TRN2", target_bir_lowering=False)

    x_in = nc.dram_tensor("x_in", [T, D], F32, kind="ExternalInput").ap()
    wq_d = nc.dram_tensor("wq_t", [NH, 128, DC * 128], F32, kind="ExternalInput").ap()
    wk_d = nc.dram_tensor("wk_t", [NKV, 128, DC * 128], F32, kind="ExternalInput").ap()
    wv_d = nc.dram_tensor("wv_t", [DC, 128, NKV * HD], F32, kind="ExternalInput").ap()
    wo_d = nc.dram_tensor("wo_t", [DC, 128, DC * 128], F32, kind="ExternalInput").ap()
    wra_d = nc.dram_tensor("wra", [DC, 128, NEXP + HR], F32, kind="ExternalInput").ap()
    btf_d = nc.dram_tensor("btf", [HR, D], F32, kind="ExternalInput").ap()
    br_d = nc.dram_tensor("brrow", [1, NEXP + HR], F32, kind="ExternalInput").ap()
    cos_d = nc.dram_tensor("cos_t", [HD, T], F32, kind="ExternalInput").ap()
    ssin_d = nc.dram_tensor("ssin_t", [HD, T], F32, kind="ExternalInput").ap()
    idn_d = nc.dram_tensor("idn", [128, 128], F32, kind="ExternalInput").ap()
    out_d = nc.dram_tensor("out", [T, D], F32, kind="ExternalOutput").ap()
    if DEBUG_DUMPS:
        dbg_x1 = nc.dram_tensor("dbg_x1", [T, D], F32, kind="ExternalOutput").ap()
        dbg_klv = nc.dram_tensor("dbg_klv", [2048, 1024], BF16, kind="ExternalOutput").ap()
        dbg_q = nc.dram_tensor("dbg_q", [NH * 128, T], BF16, kind="ExternalOutput").ap()
        dbg_xat = nc.dram_tensor("dbg_xat", [T, D], F32, kind="ExternalOutput").ap()
        dbg_ot = nc.dram_tensor("dbg_ot", [128, 32768], BF16, kind="ExternalOutput").ap()

    klv_loc_t = nc.dram_tensor("klv_loc", [2048, 1024], BF16)
    klv_g_t = nc.dram_tensor("klv_g", [NCORE * 2048 * 2, 1024], BF16)
    klv_loc = klv_loc_t.ap()
    klv_g_full = klv_g_t.ap()
    klv_g = klv_g_full[0:NCORE * 2048, :]
    q_loc = nc.dram_tensor("q_loc", [NH * 128, T], BF16).ap()
    x1_d = nc.dram_tensor("x1_d", [T, D], F32).ap()
    xat_d = nc.dram_tensor("xat_d", [T, D], F32).ap()

    from contextlib import ExitStack
    ctx = ExitStack()
    nc._kernel_ctx = ctx

    sbt = lambda name, shape, dt: ctx.enter_context(nc.sbuf_tensor(name, shape, dt))
    pst = lambda name, shape, dt: ctx.enter_context(nc.psum_tensor(name, shape, dt))

    blob1 = sbt("blob1", [128, 32768], BF16)
    blob2 = sbt("blob2", [128, 32768], BF16)
    kb0 = sbt("kb0", [128, 8192], BF16)
    kb1 = sbt("kb1", [128, 8192], BF16)
    qb = sbt("qb", [128, 8704], BF16)
    xb0 = sbt("xb0", [128, 4096], F32)
    ptile_t = sbt("ptile", [128, 3072], BF16)
    idn32 = sbt("idn32", [128, 128], F32)
    idnbf = sbt("idnbf", [128, 128], BF16)
    st_a = sbt("st_a", [128, 512], F32)
    st_b = sbt("st_b", [128, 512], F32)
    zinv_t = sbt("zinv", [128, 4], F32)
    onesb = sbt("onesb", [128, 1], BF16)
    onesf = sbt("onesf", [128, 1], F32)
    ones1b = sbt("ones1b", [1, 128], BF16)
    ones1f = sbt("ones1f", [1, 128], F32)
    scr8 = sbt("scr8", [128, 8], F32)
    stat = sbt("stat", [128, 16], F32)
    consts = sbt("consts", [128, 8], F32)

    CONST_VALS = [-EXP_BIAS, INV_SQRT_HD, LN_EPS, -0.5]
    for i, v in enumerate(CONST_VALS):
        nc.const_aps.register(F32, float(v), consts[:, i:i + 1])

    cosb = xb0.ap()[:, 0:1024]
    ssinb = xb0.ap()[:, 1024:2048]
    b2 = blob2.ap()
    qkst = st_a.ap().bitcast(BF16)
    lowS = b2[:, 16384:16512]
    lowST = b2[:, 16512:16640]
    ln_tok = b2[:, 0:4096]
    lnt_tile = b2[:, 4096:8192]
    x1t = b2[:, 8192:16384].bitcast(F32)
    otall = b2
    qbap = qb.ap()
    wra_sb = qbap[:, 0:DC * 136]
    btf_sb = qbap[:, DC * 136:DC * 136 + 4096]
    brr_sb = qbap[:, DC * 136 + 4096:DC * 136 + 4096 + 136]
    rope_c1 = qbap[:, 0:2048].bitcast(F32)
    rope_s1 = qbap[:, 2048:4096].bitcast(F32)
    qslot = [qbap[:, 0:4096], qbap[:, 4096:8192]]
    x2t = kb0.ap().bitcast(F32)
    xatt = kb1.ap().bitcast(F32)
    ptile = ptile_t.ap()
    zinv = zinv_t.ap()
    acc_sb = xb0.ap()[:, 2048:2816]
    zrow = st_a.ap()[0:1, 0:512]
    otc = st_b.ap()
    blob1f = blob1.ap().bitcast(F32)
    kb0f = kb0.ap().bitcast(F32)
    kb1f = kb1.ap().bitcast(F32)
    ln_tok1 = b2[:, 20480:24576]
    lnt_tile1 = b2[:, 24576:28672]

    ps_s = pst("ps_s", [128, 3072], F32).ap()
    ps_ot = pst("ps_ot", [128, 512], F32).ap()
    ps_z = pst("ps_z", [128, 512], F32).ap()
    ps_tr = [ps_s[:, 1536:2048].bitcast(BF16), ps_s[:, 2048:2560].bitcast(BF16)]
    ps_zb = ps_s[:, 2048:2560]
    ps_lw = ps_ot.bitcast(BF16)

    S = {}
    for name in ["q_x", "q_w", "q_sp", "c_pe", "c_pes", "c_pev", "c_exp",
                 "c_se", "c_ve", "c_veo", "c_vez", "cc", "cc2"]:
        S[name] = Sem(ctx.enter_context(nc.semaphore(name)))
    q_x, q_w, q_sp = S["q_x"], S["q_w"], S["q_sp"]
    c_pe, c_pes, c_pev = S["c_pe"], S["c_pes"], S["c_pev"]
    c_exp, c_se, c_ve, c_veo = S["c_exp"], S["c_se"], S["c_ve"], S["c_veo"]
    cc = S["cc"]
    cc2 = S["cc2"]
    c_vez = S["c_vez"]

    prog = {k: [] for k in ["sync", "gpsimd", "tensor", "vector", "scalar"]}

    def on(eng, fn):
        prog[eng].append(fn)

    def op(eng, fn, inc=None, k=1):
        if inc is not None:
            h = inc.inc(k)
            on(eng, lambda e, fn=fn, h=h, k=k: fn(e).then_inc(h, k))
        else:
            on(eng, lambda e, fn=fn: fn(e))

    def dma(eng, sem, out_ap, in_ap):
        op(eng, lambda e, o=out_ap, i=in_ap: e.dma_start(out=o, in_=i), inc=sem, k=16)

    _last_wait = {}
    _future_waits = []

    def wait(eng, sem, val=None, future=False):
        v = sem.n if val is None else val
        if future:
            _future_waits.append((eng, sem, v))
        else:
            assert v <= sem.n, f"future wait on {eng}: {v} > {sem.n}"
        key = (eng, id(sem))
        if v > 0 and _last_wait.get(key, 0) < v:
            _last_wait[key] = v
            on(eng, lambda e, h=sem.h, v=v: e.wait_ge(h, v))

    stg_last = {"a": 0, "b": 0}

    op("vector", lambda v: v.memset(onesb[:], 1.0))
    op("vector", lambda v: v.memset(onesf[:], 1.0))
    op("vector", lambda v: v.memset(ones1b[:], 1.0))
    op("vector", lambda v: v.memset(ones1f[:], 1.0))
    for i, val in enumerate(CONST_VALS):
        op("vector", lambda v, i=i, val=val: v.memset(consts[:, i:i + 1], float(val)))
    op("vector", lambda v: v.memset(brr_sb[0:1, :], 0.0), inc=c_ve)

    dma("sync", q_x, idn32[:], idn_d)
    wait("vector", q_x)
    op("vector", lambda v: v.tensor_copy(idnbf[:], idn32[:]), inc=c_ve)
    VE_SETUP = c_ve.n

    for dc in range(DC):
        dma("gpsimd", q_w, wra_sb[:, dc * 136:(dc + 1) * 136], wra_d[dc])
    dma("gpsimd", q_w, btf_sb, btf_d)
    wait("gpsimd", c_ve, VE_SETUP)
    dma("gpsimd", q_w, brr_sb[0:1, 0:136], br_d)
    QW_FFN = q_w.n


    def emit_ln(src_f32, dst_bf, extra_scalar_wait=None, dst_war=None):
        sums, ssq = stat[:, 0:1], stat[:, 1:2]
        mcol, msq = stat[:, 2:3], stat[:, 3:4]
        vcol, lncol = stat[:, 4:5], stat[:, 5:6]
        rstd, nmr = stat[:, 6:7], stat[:, 7:8]
        if extra_scalar_wait is not None:
            extra_scalar_wait()
        if dst_war is not None:
            wait("scalar", c_pe, dst_war)
            wait("vector", c_pe, dst_war)
        wait("scalar", c_ve)
        op("scalar", lambda s: s.activation(dst_bf, src_f32, AF.Identity, accum_out=sums), inc=c_se)
        op("scalar", lambda s: s.activation(dst_bf, src_f32, AF.Square, accum_out=ssq), inc=c_se)
        wait("vector", c_se)
        op("vector", lambda v: v.tensor_scalar(mcol, sums, 1.0 / D, None, ALU.mult), inc=c_ve)
        op("vector", lambda v: v.tensor_scalar(msq, ssq, 1.0 / D, None, ALU.mult), inc=c_ve)
        wait("vector", c_ve)
        op("vector", lambda v: v.tensor_mul(vcol, mcol, mcol), inc=c_ve)
        wait("vector", c_ve)
        op("vector", lambda v: v.tensor_sub(vcol, msq, vcol), inc=c_ve)
        wait("scalar", c_ve)
        op("scalar", lambda s: s.activation(lncol, vcol, AF.Ln, bias=LN_EPS), inc=c_se)
        wait("scalar", c_se)
        op("scalar", lambda s: s.activation(rstd, lncol, AF.Exp, scale=-0.5), inc=c_se)
        wait("vector", c_se)
        op("vector", lambda v: v.tensor_scalar(nmr, mcol, rstd, -1.0, ALU.mult, ALU.mult), inc=c_ve)
        wait("vector", c_ve)
        op("vector", lambda v: v.tensor_scalar(dst_bf, src_f32, rstd, nmr, ALU.mult, ALU.add), inc=c_ve)

    def emit_transposes(src_bf, dst_fn, evac_war=None):
        src_ready = c_ve.n
        wait("tensor", c_veo)
        if evac_war is not None:
            wait("vector", c_pe, evac_war)
        for g in range(4):
            slot = ps_tr[g % 2]
            wait("tensor", c_ve, src_ready)
            if g >= 2:
                wait("tensor", c_veo)
            for j in range(8):
                dc = g * 8 + j
                kwargs = dict(inc=c_pe) if j == 7 else {}
                op("tensor", lambda t, dc=dc, j=j, slot=slot:
                    t.transpose(slot[:, j * 128:(j + 1) * 128],
                                src_bf[:, dc * 128:(dc + 1) * 128], idnbf[:]), **kwargs)
            wait("vector", c_pe)
            op("vector", lambda v, g=g, slot=slot:
                v.tensor_copy(dst_fn(g), slot.rearrange("p (a b) -> p a b", a=8)),
                inc=c_veo)

    def emit_ffn(lnt, out_writer, qw_target):
        rt = ps_s[:, 0:136]
        wait("tensor", c_ve)
        wait("tensor", c_veo)
        wait("tensor", q_w, qw_target)
        for dc in range(DC):
            op("tensor", lambda t, dc=dc:
                t.matmul(rt, lnt[:, dc * 128:(dc + 1) * 128],
                         wra_sb[:, dc * 136:(dc + 1) * 136], start=(dc == 0), stop=False))
        op("tensor", lambda t:
            t.matmul(rt, ones1b[0:1, :], brr_sb[0:1, 0:136], start=False, stop=True), inc=c_pe)
        rt_done = c_pe.n
        mx, nmx, sm, rs, sc2 = stat[:, 8:9], stat[:, 9:10], stat[:, 10:11], stat[:, 11:12], stat[:, 12:13]
        wait("vector", c_pe)
        op("vector", lambda v: v.reduce_max(mx, rt[:, 0:8], axis=AX.X), inc=c_ve)
        wait("vector", c_ve)
        op("vector", lambda v: v.tensor_scalar(nmx, mx, -1.0, None, ALU.mult), inc=c_ve)
        wait("scalar", c_ve)
        op("scalar", lambda s: s.activation(scr8[:], rt[:, 0:8], AF.Exp, bias=nmx), inc=c_se)
        wait("vector", c_se)
        op("vector", lambda v: v.reduce_sum(sm, scr8[:], axis=AX.X), inc=c_ve)
        wait("vector", c_ve)
        op("vector", lambda v: v.reciprocal(rs, sm), inc=c_ve)
        for h in range(NEXP):
            wait("vector", c_ve)
            op("vector", lambda v, h=h:
                v.tensor_scalar(sc2, scr8[:, h:h + 1], rs, 8.0, ALU.mult, ALU.mult), inc=c_ve)
            wait("vector", c_ve)
            op("vector", lambda v, h=h:
                v.tensor_scalar(lowS[:, h * 16:(h + 1) * 16], rt[:, 8 + h * 16:8 + (h + 1) * 16],
                                sc2, None, ALU.mult), inc=c_ve)
        wait("tensor", c_ve)
        op("tensor", lambda t: t.transpose(ps_lw[:, 0:128], lowS, idnbf[:]), inc=c_pe)
        wait("vector", c_pe)
        op("vector", lambda v: v.tensor_copy(lowST, ps_lw[:, 0:128]), inc=c_ve)
        wait("tensor", c_ve)
        evac_done = {}
        for oc in range(8):
            slot = ps_s[:, 512 + (oc % 2) * 512:1024 + (oc % 2) * 512]
            if oc >= 2:
                wait("tensor", c_ve, evac_done[oc - 2])
            op("tensor", lambda t, slot=slot, oc=oc:
                t.matmul(slot, lowST, btf_sb[:, oc * 512:(oc + 1) * 512],
                         start=True, stop=True), inc=c_pe)
            wait("vector", c_pe)
            out_writer(oc, slot)
            evac_done[oc] = c_ve.n
        return rt_done

    lnb = blob1.ap()
    xs_p1 = [xb0.ap()[:, 0:4096], kb1f[:, 0:4096]]
    x1s_p1 = [x1t, kb0f[:, 0:4096]]
    lns = [(ln_tok, lnt_tile), (ln_tok1, lnt_tile1)]
    p1_x_cons = {}
    p1_tr2_done = {}
    p1_rt_done = {}

    for tt in range(TT):
        s = tt % 2
        xcur, x1cur = xs_p1[s], x1s_p1[s]
        lt, ltt = lns[s]
        if tt >= 2:
            wait("sync", c_ve, p1_x_cons[tt - 2])
        dma("sync", q_x, xcur, x_in[tt * 128:(tt + 1) * 128, :])
        wait("vector", q_x)
        wait("scalar", q_x)
        if tt == 0:
            wait("scalar", c_ve, VE_SETUP)
        emit_ln(xcur, lt, dst_war=p1_tr2_done.get(tt - 2))
        emit_transposes(lt, lambda g, ltt=ltt: ltt[:, g * 1024:(g + 1) * 1024]
                        .rearrange("p (a b) -> p a b", a=8),
                        evac_war=p1_rt_done.get(tt - 2))

        def p1_writer(oc, slot, tt=tt, xcur=xcur, x1cur=x1cur):
            key = "a" if oc % 2 == 0 else "b"
            st = st_a if oc % 2 == 0 else st_b
            op("vector", lambda v, oc=oc, slot=slot:
                v.scalar_tensor_tensor(x1cur[:, oc * 512:(oc + 1) * 512], slot, FFN_COEF,
                                       xcur[:, oc * 512:(oc + 1) * 512], ALU.mult, ALU.add),
                inc=c_ve)
            wait("vector", c_ve)
            wait("vector", q_sp, stg_last[key])
            op("vector", lambda v, st=st, oc=oc:
                v.tensor_copy(st[:], x1cur[:, oc * 512:(oc + 1) * 512]), inc=c_ve)
            wait("sync", c_ve)
            dma("sync", q_sp, x1_d[tt * 128:(tt + 1) * 128, oc * 512:(oc + 1) * 512], st[:])
            stg_last[key] = q_sp.n
        p1_rt_done[tt] = emit_ffn(ltt, p1_writer, QW_FFN)
        p1_x_cons[tt] = c_ve.n

        emit_ln(x1cur, lt)

        def lnb_dst(g, tt=tt):
            return lnb.rearrange("p (dc t) -> p dc t", dc=DC)[:, g * 8:(g + 1) * 8,
                                                             tt * 128:(tt + 1) * 128]
        emit_transposes(lt, lnb_dst)
        p1_tr2_done[tt] = c_pe.n

    VE_P1 = c_ve.n
    VEO_P1 = c_veo.n

    wvt = blob2.ap()
    wait("gpsimd", c_ve, VE_P1)
    wait("gpsimd", c_veo, VEO_P1)
    for dc in range(DC):
        dma("gpsimd", q_w, wvt[:, dc * 1024:(dc + 1) * 1024], wv_d[dc])
    QW_WVT = q_w.n

    def v_bank(t2):
        if t2 < 6:
            return ps_s[:, t2 * 512:(t2 + 1) * 512]
        if t2 == 6:
            return ps_ot
        return ps_z

    for vh in range(2):
        wait("tensor", q_w, QW_WVT)
        wait("tensor", c_veo, VEO_P1)
        wait("tensor", c_ve)
        for dc in range(DC):
            for t2 in range(TT):
                kwargs = dict(inc=c_pe) if (dc == DC - 1 and t2 == TT - 1) else {}
                op("tensor", lambda t, dc=dc, t2=t2, vh=vh:
                    t.matmul(v_bank(t2),
                             lnb[:, dc * 1024 + t2 * 128:dc * 1024 + (t2 + 1) * 128],
                             wvt[:, dc * 1024 + vh * 512:dc * 1024 + (vh + 1) * 512],
                             start=(dc == 0), stop=(dc == DC - 1)), **kwargs)
        wait("vector", c_pe)
        for t2 in range(TT):
            key = "a" if t2 % 2 == 0 else "b"
            st = st_a if t2 % 2 == 0 else st_b
            wait("vector", q_sp, stg_last[key])
            op("vector", lambda v, t2=t2, st=st:
                v.tensor_copy(st[:].bitcast(BF16)[:, 0:512], v_bank(t2)), inc=c_ve)
            wait("sync", c_ve)
            dma("sync", q_sp,
                klv_loc[1024 + t2 * 128:1024 + (t2 + 1) * 128, vh * 512:(vh + 1) * 512],
                st[:].bitcast(BF16)[:, 0:512])
            stg_last[key] = q_sp.n

    wait("sync", c_ve, VE_P1)
    dma("sync", q_x, cosb, cos_d)
    dma("sync", q_x, ssinb, ssin_d)
    wblk = [kb0.ap()[:, 0:4096], kb1.ap()[:, 0:4096]]
    rope_free = {0: 0, 1: 0}
    blk_free = {0: 0, 1: 0}
    qk_store_last = 0

    def emit_qk_head(par, w_src, dst_rows, first):
        nonlocal qk_store_last
        wait("gpsimd", c_pe, blk_free[par])
        dma("gpsimd", q_w, wblk[par], w_src)
        qwn = q_w.n
        psv = ps_s[:, par * 1024:(par + 1) * 1024]
        wait("tensor", q_w, qwn)
        if first:
            wait("tensor", c_ve)
            wait("tensor", c_veo, VEO_P1)
        wait("tensor", c_ve, rope_free[par])
        for half in range(2):
            for dc in range(DC):
                kwargs = dict(inc=c_pe) if (dc == DC - 1 and half == 1) else {}
                op("tensor", lambda t, dc=dc, half=half, par=par:
                    t.matmul(psv[:, half * 512:(half + 1) * 512],
                             wblk[par][:, dc * 128:(dc + 1) * 128],
                             lnb[:, dc * 1024 + half * 512:dc * 1024 + (half + 1) * 512],
                             start=(dc == 0), stop=(dc == DC - 1)), **kwargs)
        blk_free[par] = c_pe.n
        wait("vector", c_pe)
        wait("vector", q_x)
        op("vector", lambda v, par=par: v.tensor_mul(rope_c1, psv, cosb), inc=c_ve)
        op("vector", lambda v, par=par:
            v.tensor_mul(rope_s1[0:64, :], psv[64:128, :], ssinb[0:64, :]), inc=c_ve)
        op("vector", lambda v, par=par:
            v.tensor_mul(rope_s1[64:128, :], psv[0:64, :], ssinb[64:128, :]), inc=c_ve)
        wait("vector", c_ve)
        wait("vector", q_sp, max(qk_store_last, stg_last["a"]))
        op("vector", lambda v: v.tensor_add(qkst, rope_c1, rope_s1), inc=c_ve)
        rope_free[par] = c_ve.n
        wait("sync", c_ve)
        dma("sync", q_sp, dst_rows, qkst)
        qk_store_last = q_sp.n

    for kh in range(NKV):
        emit_qk_head(kh % 2, wk_d[kh], klv_loc[kh * 128:(kh + 1) * 128, :], first=(kh == 0))
    KLV_DONE = q_sp.n

    for qh in range(NH):
        emit_qk_head(qh % 2, wq_d[qh], q_loc[qh * 128:(qh + 1) * 128, :], first=False)

    wait("gpsimd", q_sp, KLV_DONE)
    op("gpsimd", lambda g:
        g.collective_compute("AllGather", ALU.bypass, replica_groups=[list(range(NCORE))],
                             ins=[klv_loc[0:1024, :].opt()],
                             outs=[klv_g_full[0:NCORE * 1024, :].opt()]), inc=cc)
    op("gpsimd", lambda g:
        g.collective_compute("AllGather", ALU.bypass, replica_groups=[list(range(NCORE))],
                             ins=[klv_loc[1024:2048, :].opt()],
                             outs=[klv_g_full[NCORE * 1024:NCORE * 2048, :].opt()]), inc=cc2)

    PE_P2 = c_pe.n
    VE_P2 = c_ve.n
    QSP_P2 = q_sp.n

    def v_view(h, kt):
        base = ((h // 2) % 2) * 16384
        return blob1.ap()[:, base + kt * 256 + (h % 2) * 128:base + kt * 256 + (h % 2) * 128 + 128]

    def k_view(h, kt):
        return (kb0 if h % 2 == 0 else kb1).ap()[:, kt * 128:(kt + 1) * 128]

    def load_v_gang(g):
        base = ((g % 2)) * 16384
        for r in range(NCORE):
            src = klv_g[NCORE * 1024 + r * 1024:NCORE * 1024 + (r + 1) * 1024,
                        g * 256:(g + 1) * 256]
            src = src.rearrange("(kt p) c -> p kt c", p=128)
            dst = blob1.ap()[:, base + r * 8 * 256:base + (r + 1) * 8 * 256]
            dst = dst.rearrange("p (kt c) -> p kt c", c=256)
            dma("sync", q_x, dst, src)

    def load_k_head(h):
        dst = (kb0 if h % 2 == 0 else kb1).ap()
        for r in range(NCORE):
            dma("sync", q_x, dst[:, r * 1024:(r + 1) * 1024],
                klv_g[r * 1024 + h * 128:r * 1024 + (h + 1) * 128, :])

    def load_q_head(h):
        dma("sync", q_x, qslot[h % 2].rearrange("p (q t) -> p q t", q=4),
            q_loc[h * 4 * 128:(h + 1) * 4 * 128, :].rearrange("(q p) t -> p q t", p=128))

    wait("sync", cc, 1)
    wait("sync", c_pe, PE_P2)
    wait("sync", c_ve, VE_P2)
    wait("sync", q_sp, QSP_P2)
    load_k_head(0)
    load_q_head(0)
    kq_ready = {0: q_x.n}
    wait("sync", cc2, 1)
    load_v_gang(0)
    vg_ready = {0: q_x.n}
    load_v_gang(1)
    vg_ready[1] = q_x.n
    head_loads = {0: q_x.n}
    pending_tail = []
    otc_free = {}
    head_s_done = {}
    gang_pv_done = {}
    ot_free = {}
    unit = 0

    def emit_pv_z(h, ktd, exp_base):
        j = ktd // 2
        prhs = ptile[:, (j % 3) * 1024 + (ktd % 2) * 512:(j % 3) * 1024 + (ktd % 2) * 512 + 512]
        op("tensor", lambda t, h=h, ktd=ktd, prhs=prhs:
            t.matmul(ps_ot, v_view(h, ktd), prhs, start=(ktd == 0), stop=(ktd == KT - 1)))
        for j2 in range(4):
            kwargs = dict(inc=c_pev) if (j2 == 3 and ktd % 2 == 1) else {}
            op("tensor", lambda t, j2=j2, prhs=prhs, ktd=ktd:
                t.matmul(ps_z[:, j2:j2 + 1], prhs[:, j2 * 128:(j2 + 1) * 128], onesb[:],
                         start=False, stop=(ktd == KT - 1 and j2 == 3)),
                **kwargs)

    def emit_s(h, qv, kt, exp_base, u):
        if kt >= 6:
            wait("tensor", c_exp, exp_base + (kt - 6) // 2 + 1, future=True)
        else:
            wait("tensor", c_exp, exp_base)
        if u == 0 and kt == 0:
            wait("tensor", c_ve, VE_P2)
        sv = ps_s[:, (kt % 6) * 512:(kt % 6) * 512 + 512]
        kwargs = dict(inc=c_pes) if kt % 2 == 1 else {}
        op("tensor", lambda t, h=h, kt=kt, sv=sv, qv=qv:
            t.matmul(sv, k_view(h, kt), qv, start=True, stop=True), **kwargs)

    for h in range(NKV):
        wait("tensor", q_x, kq_ready[h])
        if h == 0:
            wait("tensor", q_sp, QSP_P2)
        for ql in range(GROUPS):
            qh = h * GROUPS + ql
            for half in range(2):
                u = unit
                qv = qslot[h % 2][:, ql * 1024 + half * 512:ql * 1024 + half * 512 + 512]
                exp_base = c_exp.n
                pes_base = c_pes.n
                pev_base = c_pev.n
                vez_base = c_vez.n
                op("vector", lambda v: v.memset(ps_z[:, 0:4], 0.0), inc=c_ve)
                zmemset_ve = c_ve.n
                for kt in range(4):
                    emit_s(h, qv, kt, exp_base, u)
                if pending_tail:
                    pending_tail.pop(0)()
                    ot_free[u - 1] = c_veo.n
                if u >= 1:
                    wait("tensor", c_veo, ot_free[u - 1])
                for kt in (4, 5):
                    emit_s(h, qv, kt, exp_base, u)
                for ktd in range(KT):
                    if ktd == 0:
                        if u >= 1:
                            wait("tensor", c_ve, otc_free[u - 1])
                        else:
                            wait("tensor", c_veo)
                        wait("tensor", c_ve, zmemset_ve)
                        wait("tensor", q_x, vg_ready[h // 2])
                    if ktd % 2 == 0:
                        wait("tensor", c_exp, exp_base + ktd // 2 + 1, future=True)
                    emit_pv_z(h, ktd, exp_base)
                    if ktd + 6 < KT:
                        emit_s(h, qv, ktd + 6, exp_base, u)
                for j in range(KT // 2):
                    pslot = ptile[:, (j % 3) * 1024:(j % 3) * 1024 + 1024]
                    sj = ps_s[:, (2 * j % 6) * 512:(2 * j % 6) * 512 + 1024]
                    wait("scalar", c_pes, pes_base + j + 1)
                    pw = pev_base + (j - 3) + 1
                    if pw > 0:
                        wait("scalar", c_pev, pw)
                    op("scalar", lambda s, sj=sj, pslot=pslot:
                        s.activation(pslot, sj, AF.Exp, bias=-EXP_BIAS,
                                     scale=INV_SQRT_HD), inc=c_exp)
                wait("vector", c_pev, pev_base + KT // 2)
                op("vector", lambda v: v.tensor_copy(otc[:], ps_ot), inc=c_ve)
                otc_free[u] = c_ve.n
                op("vector", lambda v: v.reciprocal(zinv[:], ps_z[:, 0:4]), inc=c_ve)
                zinv_ve = c_ve.n

                def unit_tail(qh=qh, half=half, zinv_ve=zinv_ve):
                    wait("tensor", c_ve, zinv_ve)
                    for j2 in range(4):
                        kwargs = dict(inc=c_pe) if j2 == 3 else {}
                        op("tensor", lambda t, j2=j2:
                            t.transpose(ps_zb[0:1, j2 * 128:(j2 + 1) * 128],
                                        zinv[:, j2:j2 + 1], idn32[:]), **kwargs)
                    wait("vector", c_pe)
                    op("vector", lambda v: v.tensor_copy(zrow[0:1, :], ps_zb[0:1, 0:512]), inc=c_ve)
                    zrow_ve = c_ve.n
                    wait("tensor", c_ve, zrow_ve)
                    op("tensor", lambda t:
                        t.matmul(ps_zb[:, 0:512], ones1f[0:1, :], zrow[0:1, :],
                                 start=True, stop=True), inc=c_pe)
                    wait("vector", c_pe)
                    op("vector", lambda v, qh=qh, half=half:
                        v.tensor_mul(otall[:, qh * 1024 + half * 512:qh * 1024 + half * 512 + 512],
                                     otc[:], ps_zb[:, 0:512]), inc=c_veo)
                pending_tail.append(unit_tail)
                unit += 1
        head_s_done[h] = c_pes.n
        if h % 2 == 1:
            gang_pv_done[h // 2] = c_pev.n
        if h + 1 < NKV:
            nh_ = h + 1
            if nh_ >= 2:
                wait("sync", c_pes, head_s_done[nh_ - 2])
            load_k_head(nh_)
            load_q_head(nh_)
            kq_ready[nh_] = q_x.n
            if h % 2 == 1:
                g_next = h // 2 + 2
                if g_next <= 3:
                    wait("sync", c_pev, gang_pv_done[g_next - 2])
                    load_v_gang(g_next)
                    vg_ready[g_next] = q_x.n

    if pending_tail:
        pending_tail.pop(0)()
        ot_free[unit - 1] = c_veo.n
    PE_P3 = c_pe.n
    PES_P3 = c_pes.n
    VEO_P3 = c_veo.n

    wo_blk_free = {0: ("pes", head_s_done[NKV - 2]), 1: ("pes", head_s_done[NKV - 1])}
    acc_free = {}
    first_sta = True

    for doc in range(DC):
        par = doc % 2
        sem_kind, val = wo_blk_free[par]
        wait("gpsimd", c_pes if sem_kind == "pes" else c_pe, val)
        dma("gpsimd", q_w, wblk[par], wo_d[doc])
        qwn = q_w.n
        for half in range(2):
            sidx = (doc * 2 + half) % 4
            slot = ps_s[:, sidx * 512:(sidx + 1) * 512]
            wait("tensor", q_w, qwn)
            if doc * 2 + half >= 4:
                wait("tensor", c_ve, acc_free[sidx])
            elif doc == 0 and half == 0:
                wait("tensor", c_veo, VEO_P3)
                wait("tensor", c_exp)
                wait("tensor", c_pev)
            for cci in range(DC):
                kwargs = dict(inc=c_pe) if cci == DC - 1 else {}
                op("tensor", lambda t, doc=doc, half=half, cci=cci, slot=slot, par=par:
                    t.matmul(slot, wblk[par][:, cci * 128:(cci + 1) * 128],
                             otall[:, cci * 1024 + half * 512:cci * 1024 + half * 512 + 512],
                             start=(cci == 0), stop=(cci == DC - 1)), **kwargs)
            if half == 1:
                wo_blk_free[par] = ("pe", c_pe.n)
            wait("vector", c_pe)
            if first_sta:
                wait("vector", q_sp, stg_last["a"])
                first_sta = False
            op("vector", lambda v, slot=slot: v.tensor_copy(st_a[:], slot), inc=c_ve)
            acc_free[sidx] = c_ve.n
            wait("tensor", c_ve)
            trb = ps_ot if (doc * 2 + half) % 2 == 0 else ps_z
            for j in range(4):
                kwargs = dict(inc=c_pe) if j == 3 else {}
                op("tensor", lambda t, j=j, trb=trb:
                    t.transpose(trb[0:1 * 128, j * 128:(j + 1) * 128],
                                st_a[:, j * 128:(j + 1) * 128], idn32[:]), **kwargs)
            wait("vector", c_pe)
            wait("vector", q_sp, stg_last["b"])
            op("vector", lambda v, trb=trb:
                v.tensor_copy(st_b[:], trb[:, 0:512]), inc=c_ve)
            wait("sync", c_ve)
            dma("sync", q_sp,
                xat_d[half * 512:(half + 1) * 512, doc * 128:(doc + 1) * 128]
                .rearrange("(j p) c -> p j c", p=128),
                st_b[:].rearrange("p (j c) -> p j c", c=128))
            stg_last["b"] = q_sp.n

    PE_P4 = c_pe.n
    VE_P4 = c_ve.n
    QSP_P4 = q_sp.n

    wait("gpsimd", c_pes, PES_P3)
    for dc in range(DC):
        dma("gpsimd", q_w, wra_sb[:, dc * 136:(dc + 1) * 136], wra_d[dc])
    dma("gpsimd", q_w, btf_sb, btf_d)
    op("gpsimd", lambda g: g.memset(brr_sb[0:1, :], 0.0))
    dma("gpsimd", q_w, brr_sb[0:1, 0:136], br_d)
    QW_FFN2 = q_w.n
    globals_qwffn = QW_FFN2

    x1s_p5 = [xb0.ap()[:, 0:4096], blob1f[:, 0:4096]]
    xas_p5 = [xatt[:, 0:4096], blob1f[:, 4096:8192]]
    x2s_p5 = [x2t[:, 0:4096], blob1f[:, 8192:12288]]
    p5_x_cons = {}
    p5_tr_done = {}
    p5_rt_done = {}

    for tt in range(TT):
        s = tt % 2
        x1cur, xacur, x2cur = x1s_p5[s], xas_p5[s], x2s_p5[s]
        lt, ltt = lns[s]
        if tt < 2:
            wait("sync", c_pe, PE_P4)
            wait("sync", c_ve, VE_P4)
            wait("sync", q_sp, QSP_P4)
        else:
            wait("sync", c_ve, p5_x_cons[tt - 2])
        dma("sync", q_x, x1cur, x1_d[tt * 128:(tt + 1) * 128, :])
        if tt < 2:
            wait("scalar", c_pe, PE_P4)
            wait("scalar", c_ve, VE_P4)
            wait("scalar", q_sp, QSP_P4)
        else:
            wait("scalar", c_ve, p5_x_cons[tt - 2])
        dma("scalar", q_x, xacur, xat_d[tt * 128:(tt + 1) * 128, :])
        wait("vector", q_x)
        wait("vector", q_w, QW_FFN2)
        if tt < 2:
            wait("vector", c_pe, PE_P4)
        op("vector", lambda v, a=x1cur, b=xacur, o=x2cur:
            v.tensor_add(o, a, b), inc=c_ve)
        p5_x_cons[tt] = c_ve.n
        wait("scalar", c_ve)
        wait("scalar", q_w, QW_FFN2)
        if tt < 2:
            wait("scalar", c_pe, PE_P4)
        emit_ln(x2cur, lt, dst_war=p5_tr_done.get(tt - 2))
        emit_transposes(lt, lambda g, ltt=ltt: ltt[:, g * 1024:(g + 1) * 1024]
                        .rearrange("p (a b) -> p a b", a=8),
                        evac_war=p5_rt_done.get(tt - 2))
        p5_tr_done[tt] = c_pe.n

        def p5_writer(oc, slot, tt=tt, x2cur=x2cur):
            key = "a" if oc % 2 == 0 else "b"
            st = st_a if oc % 2 == 0 else st_b
            wait("vector", q_sp, stg_last[key])
            op("vector", lambda v, oc=oc, slot=slot:
                v.scalar_tensor_tensor(st[:], slot, FFN_COEF,
                                       x2cur[:, oc * 512:(oc + 1) * 512], ALU.mult, ALU.add),
                inc=c_ve)
            wait("sync", c_ve)
            dma("sync", q_sp, out_d[tt * 128:(tt + 1) * 128, oc * 512:(oc + 1) * 512], st[:])
            stg_last[key] = q_sp.n
        p5_rt_done[tt] = emit_ffn(ltt, p5_writer, QW_FFN2)

    if DEBUG_DUMPS:
        q_dbg = Sem(ctx.enter_context(nc.semaphore("q_dbg")))
        wait("gpsimd", q_sp)
        wait("gpsimd", c_veo)
        wait("gpsimd", c_ve)
        dma("gpsimd", q_dbg, dbg_x1, x1_d)
        dma("gpsimd", q_dbg, dbg_klv, klv_loc)
        dma("gpsimd", q_dbg, dbg_q, q_loc)
        dma("gpsimd", q_dbg, dbg_xat, xat_d)
        dma("gpsimd", q_dbg, dbg_ot, otall)
        wait("gpsimd", q_dbg)
    wait("gpsimd", q_sp)

    for eng_, sem_, v_ in _future_waits:
        assert v_ <= sem_.n, f"unsatisfiable future wait {eng_} {v_} > {sem_.n}"

    with nc.Block() as block:
        @block.sync
        def _(eng):
            for fn in prog["sync"]:
                fn(eng)

        @block.gpsimd
        def _(eng):
            for fn in prog["gpsimd"]:
                fn(eng)

        @block.tensor
        def _(eng):
            for fn in prog["tensor"]:
                fn(eng)

        @block.vector
        def _(eng):
            for fn in prog["vector"]:
                fn(eng)

        @block.scalar
        def _(eng):
            for fn in prog["scalar"]:
                fn(eng)

    return nc


def _host_prep(L0_attn_hidden_states, L0_attn_Wq, L0_attn_Wk, L0_attn_Wv,
               L0_attn_Wo, L0_ffn_A, L0_ffn_B, L0_ffn_Wr, L0_ffn_br):
    f32 = np.float32
    x = np.ascontiguousarray(np.asarray(L0_attn_hidden_states, f32).reshape(8192, D))
    Wq = np.asarray(L0_attn_Wq, f32)
    Wk = np.asarray(L0_attn_Wk, f32)
    Wv = np.asarray(L0_attn_Wv, f32)
    Wo = np.asarray(L0_attn_Wo, f32)
    A = np.asarray(L0_ffn_A, f32)
    B = np.asarray(L0_ffn_B, f32)
    Wr = np.asarray(L0_ffn_Wr, f32)
    br = np.asarray(L0_ffn_br, f32)

    wq_t = np.ascontiguousarray(
        Wq.T.reshape(DC, 128, NH, 128).transpose(2, 1, 0, 3).reshape(NH, 128, DC * 128))
    wk_t = np.ascontiguousarray(
        Wk.T.reshape(DC, 128, NKV, 128).transpose(2, 1, 0, 3).reshape(NKV, 128, DC * 128))
    wv_t = np.ascontiguousarray(Wv.T.reshape(DC, 128, NKV * HD))
    wo_t = np.ascontiguousarray(
        Wo.T.reshape(DC, 128, DC, 128).transpose(2, 1, 0, 3).reshape(DC, 128, DC * 128))
    wra = np.ascontiguousarray(
        np.concatenate([Wr.T, A.transpose(2, 0, 1).reshape(D, HR)], axis=1)
        .reshape(DC, 128, NEXP + HR))
    btf = np.ascontiguousarray(B.transpose(0, 2, 1).reshape(HR, D))
    brrow = np.zeros((1, NEXP + HR), f32)
    brrow[0, :NEXP] = br

    pos = np.arange(8192, dtype=np.float64)
    inv = 1.0 / (500000.0 ** (np.arange(0, HD, 2, dtype=np.float64) / HD))
    freqs = np.outer(pos, inv)
    emb = np.concatenate([freqs, freqs], axis=1)
    cosT = np.cos(emb).T.astype(f32)
    sinT = np.sin(emb).T.astype(f32)
    ssinT = sinT.copy()
    ssinT[:64, :] *= -1.0
    idn = np.eye(128, dtype=f32)

    per_core = []
    for i in range(NCORE):
        sl = slice(i * T, (i + 1) * T)
        per_core.append({
            "x_in": np.ascontiguousarray(x[sl]),
            "wq_t": wq_t, "wk_t": wk_t, "wv_t": wv_t, "wo_t": wo_t,
            "wra": wra, "btf": btf, "brrow": brrow,
            "cos_t": np.ascontiguousarray(cosT[:, sl]),
            "ssin_t": np.ascontiguousarray(ssinT[:, sl]),
            "idn": idn,
        })
    return per_core


_NC_CACHE = {}


def kernel(**inputs):
    in_maps = _host_prep(**inputs)
    if "nc" not in _NC_CACHE:
        _NC_CACHE["nc"] = build_nc()
    nc = _NC_CACHE["nc"]
    res = run_bass_kernel_spmd(nc, in_maps, core_ids=list(range(NCORE)))
    out = np.concatenate([res.results[i]["out"] for i in range(NCORE)], axis=0)
    return out.reshape(1, 8192, D).astype(np.float32)


if __name__ == "__main__":
    nc = build_nc()
    print("built ok:", nc.n_instructions() if hasattr(nc, "n_instructions") else "?")
